# version 05
# baseline (speedup 1.0000x reference)
import numpy as np
import ml_dtypes
import concourse.bass as bass
import concourse.mybir as mybir
from concourse.bass_utils import run_bass_kernel_spmd

F32 = mybir.dt.float32
BF16 = mybir.dt.bfloat16
AF = mybir.ActivationFunctionType
OP = mybir.AluOpType
AX = mybir.AxisListType
bf16 = ml_dtypes.bfloat16

B = 8
L = 2048
D = 2048
T = 512
NCH = L // T
DC = 16
N = 16
R = 64
E = R + 2 * N
H = 5632
HB = H // 128
H2B = H // 256
NW1 = H * D
NW3 = D * H
SIG_MIN = 1e-37


class Plan:
    __slots__ = ("ops", "sems")

    def __init__(self):
        self.ops = []
        self.sems = {}

    def v(self, name):
        return self.sems.get(name, 0)

    def emit(self, eng, kind, waits=(), inc=None, **kw):
        ws = [(s, val) for (s, val) in waits if val > 0]
        if inc is not None:
            self.sems[inc[0]] = self.sems.get(inc[0], 0) + inc[1]
        self.ops.append((eng, kind, tuple(ws), inc, kw))


def build_nc(nchunks=NCH, upto="full", noffn=False):
    nc = bass.Bass("TRN2", target_bir_lowering=False, debug=False,
                   dynamic_dma_scratch_size=2048)

    xT = nc.dram_tensor("xT", [D, L], F32, kind="ExternalInput")
    wxT = nc.dram_tensor("wxT", [D, E], F32, kind="ExternalInput")
    wdtT = nc.dram_tensor("wdtT", [R, D], F32, kind="ExternalInput")
    alogT = nc.dram_tensor("alogT", [128, DC * N], F32, kind="ExternalInput")
    bdtT = nc.dram_tensor("bdtT", [128, DC], F32, kind="ExternalInput")
    dparT = nc.dram_tensor("dparT", [128, DC], F32, kind="ExternalInput")
    w1b = nc.dram_tensor("w1b", [H2B, D, 256], F32, kind="ExternalInput")
    w2b = nc.dram_tensor("w2b", [H2B, D, 256], F32, kind="ExternalInput")
    w3b = nc.dram_tensor("w3b", [DC, H, 128], F32, kind="ExternalInput")
    outT = nc.dram_tensor("outT", [D, L], F32, kind="ExternalOutput")
    q12s = nc.dram_tensor("q12s", [HB, 2, D, 128], BF16, kind="Internal")
    q3s = nc.dram_tensor("q3s", [DC, H, 128], BF16, kind="Internal")
    xscr = nc.dram_tensor("xscr", [2, 128, DC, T], F32, kind="Internal")

    xT_v = xT.rearrange("(db p) t -> p db t", p=128)
    outT_v = outT.rearrange("(db p) t -> p db t", p=128)
    w1b_v = w1b.rearrange("s (db p) h -> s p db h", p=128)
    w2b_v = w2b.rearrange("s (db p) h -> s p db h", p=128)
    w3b_v3 = w3b.rearrange("b (x hb p) d -> b x p hb d", p=128, hb=22)
    q12s_q = q12s.rearrange("(s x) m (db p) h -> s m x p db h", x=2, p=128)
    q3s_q = q3s.rearrange("b (x hb p) d -> b x p hb d", p=128, hb=22)
    q12s_f = q12s.rearrange("hc m (db p) h -> hc p m db h", p=128)
    q3s_f = q3s.rearrange("b (hb p) d -> b p hb d", p=128)
    wxT_v = wxT.rearrange("(db p) e -> p db e", p=128)
    wdtT_v = wdtT.rearrange("r (db p) -> r db p", p=128)

    sb = nc.alloc_sbuf_tensor
    x_res = sb("x_res", [128, DC, T], F32)
    BCbig = sb("BCbig", [128, 32, T], BF16)
    hv = sb("hv", [128, 16, T], BF16)
    v_bf = sb("v_bf", [128, DC, T], BF16)
    hidden = sb("hidden", [128, HB, T], BF16)
    warena = sb("warena", [128, 12288], BF16)
    dAr = sb("dAr", [128, 2, T], F32)
    dBxr = sb("dBxr", [128, 2, T], BF16)
    spx = sb("spx", [128, 2, T], F32)
    sigb = sb("sigb", [128, T], F32)
    yt = sb("yt", [128, T], F32)
    u2 = sb("u2", [128, T], F32)
    u_bf = sb("u_bf", [128, 2, T], BF16)
    xsq = sb("xsq", [128, 2, T], BF16)
    draw = sb("draw", [128, T], BF16)
    bcneg = sb("bcneg", [32, T], BF16)
    bcsp = sb("bcsp", [128, 11, T], BF16)
    mrep = sb("mrep", [128, T], F32)
    rrep = sb("rrep", [128, T], F32)
    rows = sb("rows", [1, 2, T], F32)
    Wx = sb("Wx", [128, DC, E], BF16)
    Wdt = sb("Wdt", [128, 8 * 128], BF16)
    negA = sb("negA", [128, DC * N], F32)
    negb = sb("negb", [128, DC], F32)
    dpar = sb("dpar", [128, DC], F32)
    hstate = sb("hstate", [128, DC, 2 * N], F32)
    outst = sb("outst", [128, 2, T], F32)
    xf2 = sb("xf2", [128, T], F32)
    gcol = sb("gcol", [128, 34], F32)
    thr = sb("thr", [128, 8], F32)
    gtot = sb("gtot", [1, 8], F32)
    consts = sb("consts", [128, 4], F32)
    constsb = sb("constsb", [128, 2], BF16)
    onesr = sb("onesr", [1, 128], F32)
    onesrb = sb("onesrb", [65, 128], BF16)

    Wdt_v = Wdt.reshape([128, 8, 128])
    hidf = hidden.reshape([128, HB * T]).bitcast(F32)
    hidb = hidden.reshape([128, HB * T])
    qflat = warena[:, 0:4096]
    junk = warena[:, 4096:8192]
    sgn1 = warena[:, 8192:12288]
    sgn2 = hidb[:, 16384:20480]
    warf = warena.bitcast(F32)
    wq12 = warena[:, 0:8192].rearrange("p (s m db h) -> p s m db h",
                                       s=2, m=2, db=16)
    wq3 = warena
    gat4 = outst.bitcast(BF16).reshape([128, 4, T])

    ps = nc.alloc_psum_tensor
    pA = ps("pA", [128, T], F32)
    pB = ps("pB", [128, T], F32)
    pC = ps("pC", [128, T], F32)
    pD = ps("pD", [128, T], F32)
    pE = ps("pE", [128, T], F32)
    pF = ps("pF", [128, T], F32)
    pG = ps("pG", [128, T], F32)
    pH = ps("pH", [128, T], F32)
    pdel = [pG, pH]
    prep = [pE, pF]
    pf2 = [pC, pD]

    P = Plan()
    ones_f = consts[:, 0:1]
    ones_b = constsb[:, 0:1]
    eps_ap = consts[0:1, 1:2]
    half_ap = consts[:, 2:3]

    tf1E = lambda c: max(0, 88 * (c + 1))
    tf2E = lambda c: max(0, 16 * (c + 1))
    ggE = lambda c: max(0, 44 * (c + 1))
    soutE = lambda c: max(0, 256 * (c + 1))
    vvE = lambda c: max(0, 16 * (c + 1))
    vresE = lambda c: max(0, 16 * (c + 1))
    gdbxE = lambda c: max(0, 256 * (c + 1))
    sxdE = lambda c: max(0, 16 * (c + 1))

    def init_ops():
        P.emit("sync", "dma", out=dAr[:, 0, 0:DC * N], in_=alogT.ap(),
               inc=("s_misc", 16))
        P.emit("sync", "dma", out=negb[:, :], in_=bdtT.ap(), inc=("s_misc", 16))
        P.emit("sync", "dma", out=dpar[:, :], in_=dparT.ap(), inc=("s_misc", 16))
        P.emit("sync", "dma", out=warf[:, 0:1536]
               .rearrange("p (db e) -> p db e", db=16), in_=wxT_v,
               inc=("s_cast", 16))
        P.emit("sync", "dma", out=warf[0:64, 2048:3072]
               .rearrange("p (db k) -> p db k", db=8), in_=wdtT_v[:, 0:8, :],
               inc=("s_cast", 16))
        P.emit("sync", "dma", out=warf[64:128, 2048:3072]
               .rearrange("p (db k) -> p db k", db=8), in_=wdtT_v[:, 8:16, :],
               inc=("s_cast", 16))
        P.emit("scalar", "act", waits=[("s_cast", 16)],
               out=Wx.reshape([128, DC * E]).ap(), in_=warf[:, 0:1536],
               func=AF.Copy, inc=("a_cast", 1))
        P.emit("scalar", "act", waits=[("s_cast", 48)], out=Wdt[:, :],
               in_=warf[:, 2048:3072], func=AF.Copy, inc=("a_cast", 1))
        P.emit("vector", "memset", ap=consts[:, 0:1], val=1.0)
        P.emit("vector", "memset", ap=consts[:, 1:2], val=1e-5)
        P.emit("vector", "memset", ap=consts[:, 2:3], val=0.5)
        P.emit("vector", "memset", ap=constsb[:, 0:1], val=1.0)
        P.emit("vector", "memset", ap=onesr[:, :], val=1.0)
        P.emit("vector", "memset", ap=onesrb[:, :], val=1.0)
        P.emit("vector", "memset", ap=hstate[:, :, :], val=0.0)
        P.emit("scalar", "act", waits=[("s_misc", 16)], out=negA[:, :],
               in_=dAr[:, 0, 0:DC * N], func=AF.Exp, inc=("a_init", 1))
        P.emit("vector", "ts", waits=[("s_misc", 48), ("a_init", 1)],
               out=negA[:, :], in_=negA[:, :], scalar=-1.0, op=OP.mult,
               inc=("v_init", 1))
        P.emit("sync", "dma", out=x_res[:, :, :], in_=xT_v[:, :, 0:T],
               inc=("s_x", 16))

    def phase0_gen():
        mats = [
            ("w1", [w1b_v[s] for s in range(H2B)],
             [(q12s_q[s, 0, 0], q12s_q[s, 0, 1]) for s in range(H2B)], 4096, NW1),
            ("w2", [w2b_v[s] for s in range(H2B)],
             [(q12s_q[s, 1, 0], q12s_q[s, 1, 1]) for s in range(H2B)], 4096, NW1),
            ("w3", [w3b_v3[b, x] for b in range(DC) for x in range(2)],
             [(q3s_q[b, x],) for b in range(DC) for x in range(2)], 2816, NW3),
        ]
        slabf = hidf
        for mi, (name, wv, qv, EL, numel) in enumerate(mats):
            S = len(wv)
            ga = "a_gacc_" + name
            for s in range(S):
                o = (s % 2) * 4096
                w = []
                if s >= 2:
                    w.append((ga, s - 1))
                elif mi > 0:
                    w.append(("a_q", P.v("a_q")))
                P.emit("sync", "dma", waits=w, out=slabf[:, o:o + EL],
                       in_=wv[s], inc=("s_w", 16))
                aw = [("s_w", P.v("s_w"))]
                if s == 0 and mi > 0:
                    aw.append(("v_gred", mi))
                P.emit("scalar", "act", waits=aw, out=junk[:, 0:EL],
                       in_=slabf[:, o:o + EL], func=AF.Abs,
                       accum_out=gcol[:, s:s + 1], inc=(ga, 1))
                yield
            P.emit("vector", "red", waits=[(ga, S)], out=gcol[:, 32:33],
                   in_=gcol[:, 0:S], inc=("v_gred", 1))
            P.emit("tensor", "mm", waits=[("v_gred", P.v("v_gred")),
                                          ("a_thr", P.v("a_thr"))],
                   out=pA[0:1, 0:1], lhsT=ones_f, rhs=gcol[:, 32:33],
                   start=True, stop=True, inc=("t_thr", 1))
            P.emit("vector", "ts", waits=[("t_thr", P.v("t_thr"))],
                   out=gtot[0:1, mi:mi + 1], in_=pA[0:1, 0:1],
                   scalar=float(1.0 / (2.0 * numel)), op=OP.mult,
                   inc=("v_gtot", 1))
            P.emit("tensor", "mm", waits=[("v_gtot", P.v("v_gtot"))],
                   out=pB[:, 0:1], lhsT=onesr[:, :], rhs=gtot[0:1, mi:mi + 1],
                   start=True, stop=True, inc=("t_thr", 1))
            P.emit("scalar", "act", waits=[("t_thr", P.v("t_thr"))],
                   out=thr[:, 2 * mi:2 * mi + 1], in_=pB[:, 0:1], func=AF.Copy,
                   scale=-1.0, inc=("a_thr", 1))
            P.emit("scalar", "act", out=thr[:, 2 * mi + 1:2 * mi + 2],
                   in_=pB[:, 0:1], func=AF.Copy, inc=("a_thr", 1))
            athr = P.v("a_thr")
            for s in range(S):
                o = (s % 2) * 4096
                w = [("a_q", P.v("a_q") - 2)] if s >= 2 else [(ga, S)]
                P.emit("sync", "dma", waits=w, out=slabf[:, o:o + EL],
                       in_=wv[s], inc=("s_w", 16))
                P.emit("scalar", "sign",
                       waits=[("s_w", P.v("s_w")), ("a_thr", athr)],
                       out=sgn1[:, 0:EL], in_=slabf[:, o:o + EL],
                       bias=thr[:, 2 * mi:2 * mi + 1], inc=("a_q", 1))
                P.emit("scalar", "sign", out=sgn2[:, 0:EL],
                       in_=slabf[:, o:o + EL],
                       bias=thr[:, 2 * mi + 1:2 * mi + 2], inc=("a_q", 1))
                wq = [("a_q", P.v("a_q")), ("s_qw", P.v("s_qw"))]
                if P.v("v_q") == 0:
                    wq.append(("a_cast", 2))
                P.emit("vector", "tt", waits=wq, out=qflat[:, 0:EL],
                       in0=sgn1[:, 0:EL], in1=sgn2[:, 0:EL], op=OP.add,
                       inc=("v_q", 1))
                if len(qv[s]) == 2:
                    qf = qflat.rearrange("p (db x h) -> p db x h", db=16, x=2)
                    P.emit("sync", "dma", waits=[("v_q", P.v("v_q"))],
                           out=qv[s][0], in_=qf[:, :, 0, :], inc=("s_qw", 16))
                    P.emit("sync", "dma", out=qv[s][1], in_=qf[:, :, 1, :],
                           inc=("s_qw", 16))
                else:
                    P.emit("sync", "dma", waits=[("v_q", P.v("v_q"))],
                           out=qv[s][0],
                           in_=qflat[:, 0:EL].rearrange(
                               "p (hb d) -> p hb d", hb=22),
                           inc=("s_qw", 16))
                yield

    QW_TOTAL = 16 * (2 * H2B * 2 + DC * 2)

    def mixer_gen(c):
        sx = 16 * (c + 1)
        for ln in (0, 1):
            vr = vresE(c) if ln == 1 else 0
            vo = P.v("v_row")
            for dc in range(DC):
                wq = [("s_x", sx), ("v_init", 1)]
                if ln == 1:
                    wq.append(("v_resid", vr))
                wsq = list(wq)
                if P.v("t_statq") >= 2:
                    wsq.append(("t_statq", P.v("t_statq") - 1))
                P.emit("scalar", "act", waits=wsq, out=xsq[:, dc % 2, :],
                       in_=x_res[:, dc, :], func=AF.Square, inc=("a_sq", 1))
                P.emit("tensor", "mm", waits=wq + [("v_row", vo),
                                                   ("a_lnb", P.v("a_lnb")),
                                                   ("a_rep", P.v("a_rep"))],
                       out=pE[0:1, :], lhsT=ones_f, rhs=x_res[:, dc, :],
                       start=(dc == 0), stop=(dc == DC - 1),
                       inc=(("t_stat", 1) if dc == DC - 1 else None))
                P.emit("tensor", "mm", waits=[("a_sq", P.v("a_sq"))],
                       out=pF[0:1, :], lhsT=ones_b, rhs=xsq[:, dc % 2, :],
                       start=(dc == 0), stop=(dc == DC - 1),
                       inc=("t_statq", 1))
                if dc == DC - 1:
                    P.emit("tensor", "nop", inc=("t_stat", 1))
                if dc % 4 == 3:
                    yield
            ts_ = P.v("t_stat")
            P.emit("vector", "ts", waits=[("t_stat", ts_),
                                          ("t_statq", P.v("t_statq")),
                                          ("t_bc", P.v("t_bc"))],
                   out=rows[0:1, 0, :], in_=pE[0:1, :], scalar=1.0 / D,
                   op=OP.mult)
            P.emit("vector", "tt", out=rows[0:1, 1, :], in0=rows[0:1, 0, :],
                   in1=rows[0:1, 0, :], op=OP.mult)
            P.emit("vector", "stt", out=rows[0:1, 1, :], in0=pF[0:1, :],
                   scalar=1.0 / D, in1=rows[0:1, 1, :], op0=OP.mult,
                   op1=OP.subtract, inc=("v_row", 1))
            P.emit("scalar", "act", waits=[("v_row", P.v("v_row"))],
                   out=pF[0:1, :], in_=rows[0:1, 1, :], func=AF.Sqrt,
                   bias=eps_ap, inc=("a_sqrt", 1))
            P.emit("vector", "recip", waits=[("a_sqrt", P.v("a_sqrt"))],
                   out=rows[0:1, 1, :], in_=pF[0:1, :], inc=("v_row", 1))
            P.emit("tensor", "mm", waits=[("v_row", P.v("v_row"))],
                   out=pE[:, :], lhsT=onesr[:, :], rhs=rows[0:1, 0, :],
                   start=True, stop=True, inc=("t_bc", 1))
            P.emit("tensor", "mm", out=pF[:, :], lhsT=onesr[:, :],
                   rhs=rows[0:1, 1, :], start=True, stop=True, inc=("t_bc", 1))
            P.emit("scalar", "act", waits=[("t_bc", P.v("t_bc"))],
                   out=mrep[:, :], in_=pE[:, :], func=AF.Copy, inc=("a_lnb", 1))
            P.emit("scalar", "act", out=rrep[:, :], in_=pF[:, :], func=AF.Copy,
                   inc=("a_lnb", 1))
            alnb = P.v("a_lnb")
            yield

            if ln == 0:
                for dc in range(DC):
                    wu = [("a_lnb", alnb)]
                    if dc == 0:
                        wu.append(("g_dbx", gdbxE(c - 1)))
                    if P.v("t_xdblq") >= 2:
                        wu.append(("t_xdblq", P.v("t_xdblq") - 1))
                    P.emit("vector", "tt", waits=wu, out=u2[:, :],
                           in0=x_res[:, dc, :], in1=mrep[:, :], op=OP.subtract)
                    P.emit("vector", "tt", out=u_bf[:, dc % 2, :], in0=u2[:, :],
                           in1=rrep[:, :], op=OP.mult, inc=("v_u", 1))
                    wm = [("v_u", P.v("v_u")), ("a_cast", 2)]
                    if dc == 0:
                        wm.append(("a_lnb", alnb))
                    P.emit("tensor", "mm", waits=wm, out=pE[0:64, :],
                           lhsT=Wx[:, dc, 0:64], rhs=u_bf[:, dc % 2, :],
                           start=(dc == 0), stop=(dc == DC - 1),
                           inc=(("t_xdbl", 1) if dc == DC - 1 else None))
                    P.emit("tensor", "mm", waits=[], out=pF[0:32, :],
                           lhsT=Wx[:, dc, 64:96], rhs=u_bf[:, dc % 2, :],
                           start=(dc == 0), stop=(dc == DC - 1),
                           inc=("t_xdblq", 1))
                    if dc == DC - 1:
                        P.emit("tensor", "nop", inc=("t_xdbl", 1))
                    if dc % 4 == 3:
                        yield
                txd = P.v("t_xdbl")
                P.emit("scalar", "act",
                       waits=[("t_xdbl", txd), ("t_xdblq", P.v("t_xdblq")),
                              ("t_del", P.v("t_del"))],
                       out=draw[0:64, :], in_=pE[0:64, :], func=AF.Copy,
                       inc=("a_evac", 1))
                P.emit("sync", "dma", waits=[("a_evac", P.v("a_evac")),
                                             ("t_del", P.v("t_del"))],
                       out=draw[64:128, :], in_=draw[0:64, :],
                       inc=("s_dr", 16))
                P.emit("scalar", "act", waits=[("s_bc", P.v("s_bc"))],
                       out=bcneg[0:32, :], in_=pF[0:32, :], func=AF.Copy,
                       inc=("a_evac", 1))
                aev = P.v("a_evac")
                P.emit("sync", "dma", waits=[("a_evac", aev),
                                             ("t_rep", P.v("t_rep"))],
                       out=bcsp[0:1, 0:11, :], in_=bcneg[0:11, :],
                       inc=("s_bc", 16))
                P.emit("sync", "dma", out=bcsp[32:33, 0:11, :],
                       in_=bcneg[11:22, :], inc=("s_bc", 16))
                P.emit("sync", "dma", out=bcsp[64:65, 0:10, :],
                       in_=bcneg[22:32, :], inc=("s_bc", 16))
                sbc = P.v("s_bc")
                yield
                for j in range(32):
                    base, slt = (32 * (j // 11), j % 11) if j < 22 else (64, j - 22)
                    wr = [("s_bc", sbc)]
                    if j >= 2:
                        wr.append(("a_rep", P.v("a_rep") - 1))
                    else:
                        wr.append(("a_evac", aev))
                    P.emit("tensor", "mm", waits=wr, out=prep[j % 2][:, :],
                           lhsT=onesrb[base:base + 1, :],
                           rhs=bcsp[base:base + 1, slt, :],
                           start=True, stop=True, inc=("t_rep", 1))
                    wbc = [("t_rep", P.v("t_rep"))]
                    if j == 0:
                        wbc.append(("g_dbx", gdbxE(c - 1)))
                        wbc.append(("v_resid", vresE(c - 1)))
                    P.emit("scalar", "act", waits=wbc, out=BCbig[:, j, :],
                           in_=prep[j % 2][:, :], func=AF.Copy, inc=("a_rep", 1))
                    if j % 8 == 7:
                        yield
                arep = P.v("a_rep")

                for dc in range(DC):
                    pd = pdel[dc % 2]
                    wd = [("a_evac", aev)]
                    if P.v("t_del") >= 2:
                        wd.append(("a_pd", P.v("a_pd") - 1))
                    if dc < 8:
                        dsl, drs = Wdt_v[0:64, dc, :], draw[0:64, :]
                    else:
                        dsl, drs = Wdt_v[64:128, dc - 8, :], draw[64:128, :]
                        wd.append(("s_dr", 16 * (c + 1)))
                    P.emit("tensor", "mm", waits=wd, out=pd[:, :],
                           lhsT=dsl, rhs=drs,
                           start=True, stop=True, inc=("t_del", 1))
                    P.emit("scalar", "act",
                           waits=[("t_del", P.v("t_del")), ("v_init", 1),
                                  ("v_sx", max(0, P.v("v_sx") - 1))],
                           out=sigb[:, :], in_=pd[:, :], func=AF.Relu,
                           bias=negb[:, dc:dc + 1], inc=("a_r", 1))
                    P.emit("vector", "stt", waits=[("a_r", P.v("a_r"))],
                           out=pd[:, :], in0=sigb[:, :], scalar=-2.0,
                           in1=pd[:, :], op0=OP.mult, op1=OP.add,
                           inc=("v_sx", 1))
                    we = [("v_sx", P.v("v_sx")),
                          ("v_du", max(0, P.v("v_du") - 1))]
                    P.emit("scalar", "act", waits=we, out=spx[:, dc % 2, :],
                           in_=pd[:, :], func=AF.Exp,
                           bias=negb[:, dc:dc + 1], inc=("a_pd", 1))
                    P.emit("scalar", "act", out=spx[:, dc % 2, :],
                           in_=spx[:, dc % 2, :], func=AF.Ln,
                           bias=ones_f, inc=("a_lp", 1))
                    P.emit("vector", "tt", waits=[("a_lp", P.v("a_lp"))],
                           out=spx[:, dc % 2, :], in0=sigb[:, :],
                           in1=spx[:, dc % 2, :], op=OP.add, inc=("v_sp", 1))
                    vsp = P.v("v_sp")
                    wdu = []
                    if dc < 2:
                        wdu.append(("t_xdbl", txd))
                        wdu.append(("t_xdblq", 16 * (c + 1)))
                    else:
                        wdu.append(("g_dbx", gdbxE(c - 1) + 16 * (dc - 1)))
                    P.emit("vector", "tt", out=u2[:, :], in0=x_res[:, dc, :],
                           in1=mrep[:, :], op=OP.subtract)
                    P.emit("vector", "tt", out=u2[:, :], in0=u2[:, :],
                           in1=rrep[:, :], op=OP.mult)
                    P.emit("vector", "tt", waits=wdu, out=u_bf[:, dc % 2, :],
                           in0=spx[:, dc % 2, :], in1=u2[:, :], op=OP.mult,
                           inc=("v_du", 1))
                    for n in range(N):
                        k = dc * N + n
                        gi = c * DC * N + k
                        wdA = [("v_sp", vsp)]
                        if gi >= 2:
                            wdA.append(("v_scan", P.v("v_scan") - 1))
                        P.emit("scalar", "act", waits=wdA, out=dAr[:, n % 2, :],
                               in_=spx[:, dc % 2, :], func=AF.Exp,
                               scale=negA[:, k:k + 1], inc=("a_dA", 1))
                        wbx = [("a_rep", arep)] if (dc == 0 and n == 0) else []
                        P.emit("vector", "tt", waits=wbx,
                               out=dBxr[:, n % 2, :], in0=u_bf[:, dc % 2, :],
                               in1=BCbig[:, n, :], op=OP.mult, inc=("g_dbx", 1))
                        wsc = [("a_dA", P.v("a_dA"))]
                        if c == 0 and dc == 0 and n == 0:
                            wsc.append(("v_init", 1))
                        P.emit("vector", "scan", waits=wsc, out=hv[:, n, :],
                               d0=dAr[:, n % 2, :], d1=dBxr[:, n % 2, :],
                               init=hstate[:, dc, 2 * n + 1:2 * n + 2],
                               op0=OP.mult, op1=OP.add, inc=("v_scan", 1))
                    for n in range(N):
                        P.emit("vector", "copy",
                               out=hstate[:, dc, 2 * n:2 * n + 2],
                               in_=hv[:, n, T - 2:T])
                    P.emit("vector", "tt", out=hv[:, 0:16, :],
                           in0=hv[:, 0:16, :], in1=BCbig[:, 16:32, :],
                           op=OP.mult)
                    P.emit("vector", "tt", out=hv[:, 0:8, :], in0=hv[:, 0:8, :],
                           in1=hv[:, 8:16, :], op=OP.add)
                    P.emit("vector", "tt", out=hv[:, 0:4, :], in0=hv[:, 0:4, :],
                           in1=hv[:, 4:8, :], op=OP.add)
                    P.emit("vector", "tt", out=hv[:, 0:2, :], in0=hv[:, 0:2, :],
                           in1=hv[:, 2:4, :], op=OP.add)
                    P.emit("vector", "tt", out=yt[:, :], in0=hv[:, 0, :],
                           in1=hv[:, 1, :], op=OP.add)
                    P.emit("vector", "stt", out=u2[:, :], in0=u2[:, :],
                           scalar=dpar[:, dc:dc + 1], in1=yt[:, :],
                           op0=OP.mult, op1=OP.add)
                    P.emit("vector", "tt", out=x_res[:, dc, :],
                           in0=x_res[:, dc, :], in1=u2[:, :], op=OP.add,
                           inc=("v_resid", 1))
                    yield
            else:
                for dc in range(DC):
                    wv = [("a_lnb", alnb)]
                    if dc == 0 and not noffn:
                        wv.append(("t_f1", tf1E(c - 1)))
                    P.emit("vector", "tt", waits=wv, out=u2[:, :],
                           in0=x_res[:, dc, :], in1=mrep[:, :], op=OP.subtract)
                    P.emit("vector", "tt", out=v_bf[:, dc, :], in0=u2[:, :],
                           in1=rrep[:, :], op=OP.mult, inc=("v_v", 1))
                    if dc % 4 == 3:
                        yield
        P.emit("sync", "dma", waits=[("v_v", P.v("v_v"))], out=xscr[c % 2],
               in_=x_res[:, :, :], inc=("s_xdump", 16))
        if c + 1 < nchunks:
            P.emit("sync", "dma", waits=[("s_xdump", P.v("s_xdump"))],
                   out=x_res[:, :, :], in_=xT_v[:, :, (c + 1) * T:(c + 2) * T],
                   inc=("s_x", 16))
        yield

    def ffn_gen(c):
        pend = []
        for hc in range(HB):
            slot = hc % 2
            wl = [("t_f2", tf2E(c - 1))]
            if c == 0:
                wl.append(("s_qw", QW_TOTAL))
            if P.v("t_f1") - tf1E(c - 1) >= 4:
                wl.append(("t_f1", P.v("t_f1") - 2))
            P.emit("sync", "dma", waits=wl, out=wq12[:, slot, :, :, :],
                   in_=q12s_f[hc], inc=("s_q12", 16))
            sq12 = P.v("s_q12")
            wmm = [("s_q12", sq12), ("v_v", vvE(c)),
                   ("a_f1", P.v("a_f1"))]
            if hc < 2 and c == 0:
                wmm.append(("a_thr", 6))
            for db in range(DC):
                P.emit("tensor", "mm", waits=(wmm if db == 0 else []),
                       out=pA[:, :], lhsT=wq12[:, slot, 0, db, :],
                       rhs=v_bf[:, db, :], start=(db == 0), stop=(db == DC - 1),
                       inc=(("t_f1", 1) if db == DC - 1 else None))
            for db in range(DC):
                P.emit("tensor", "mm", waits=[], out=pB[:, :],
                       lhsT=wq12[:, slot, 1, db, :], rhs=v_bf[:, db, :],
                       start=(db == 0), stop=(db == DC - 1),
                       inc=(("t_f1", 1) if db == DC - 1 else None))
            wg = [("t_f1", P.v("t_f1"))]
            if hc == 0:
                wg.append(("t_f2", tf2E(c - 1)))
            if hc < 4:
                wg.append(("s_out", soutE(c - 1)))
            else:
                wg.append(("g_gate", ggE(c - 1) + hc - 3))
            P.emit("scalar", "act", waits=wg, out=hidden[:, hc, :],
                   in_=pA[:, :], func=AF.Copy, scale=half_ap, inc=("a_f1", 1))
            P.emit("scalar", "act", out=gat4[:, hc % 4, :], in_=pB[:, :],
                   func=AF.Copy, scale=half_ap, inc=("a_f1", 1))
            pend.append(hc)
            if len(pend) == 4 or hc == HB - 1:
                for h2 in pend:
                    P.emit("scalar", "act", out=hidden[:, h2, :],
                           in_=hidden[:, h2, :], func=AF.Sigmoid,
                           inc=("a_sg", 1))
                for h2 in pend:
                    P.emit("gpsimd", "tt", waits=[("a_sg", P.v("a_sg"))],
                           out=hidden[:, h2, :], in0=hidden[:, h2, :],
                           in1=gat4[:, h2 % 4, :], op=OP.mult,
                           inc=("g_gate", 1))
                pend.clear()
            yield
        for db in range(DC):
            slot = db % 2
            wl = [("t_f1", tf1E(c))]
            if c == 0:
                wl.append(("s_qw", QW_TOTAL))
            if P.v("t_f2") - tf2E(c - 1) >= 2:
                wl.append(("t_f2", P.v("t_f2") - 1))
            q3view = wq3[:, slot * 5632:slot * 5632 + 5632].rearrange(
                "p (hb d) -> p hb d", hb=HB)
            P.emit("sync", "dma", waits=wl, out=q3view, in_=q3s_f[db],
                   inc=("s_q3", 16))
            sq3 = P.v("s_q3")
            bk = pf2[db % 2]
            wmm = [("s_q3", sq3), ("g_gate", ggE(c))]
            if db >= 2:
                wmm.append(("a_o1", P.v("a_o1") - 1))
            for hb in range(HB):
                P.emit("tensor", "mm", waits=(wmm if hb == 0 else []),
                       out=bk[:, :], lhsT=q3view[:, hb, :],
                       rhs=hidden[:, hb, :], start=(hb == 0),
                       stop=(hb == HB - 1),
                       inc=(("t_f2", 1) if hb == HB - 1 else None))
            wx2 = [("s_xdump", sxdE(c)), ("g_out", P.v("g_out"))]
            P.emit("sync", "dma", waits=wx2, out=xf2[:, :],
                   in_=xscr[c % 2][:, db, :], inc=("s_xf2", 16))
            wo1 = [("t_f2", P.v("t_f2")), ("g_gate", ggE(c)),
                   ("s_out", 16 * (16 * c + db - 1))]
            P.emit("scalar", "act", waits=wo1, out=outst[:, db % 2, :],
                   in_=bk[:, :], func=AF.Copy, scale=half_ap, inc=("a_o1", 1))
            P.emit("gpsimd", "tt", waits=[("a_o1", P.v("a_o1")),
                                          ("s_xf2", P.v("s_xf2"))],
                   out=outst[:, db % 2, :], in0=outst[:, db % 2, :],
                   in1=xf2[:, :], op=OP.add, inc=("g_out", 1))
            P.emit("sync", "dma", waits=[("g_out", P.v("g_out"))],
                   out=outT_v[:, db, c * T:(c + 1) * T],
                   in_=outst[:, db % 2, :], inc=("s_out", 16))
            yield

    def weave(gens_ratios):
        active = [[g, w] for (g, w) in gens_ratios]
        while active:
            for item in list(active):
                g, w = item
                for _ in range(w):
                    try:
                        next(g)
                    except StopIteration:
                        active.remove(item)
                        break

    init_ops()
    if upto == "phase0":
        weave([(phase0_gen(), 1)])
        P.emit("sync", "wait_only", waits=[("s_qw", P.v("s_qw"))])
    elif upto == "mixer":
        weave([(phase0_gen(), 3), (mixer_gen(0), 1)])
        P.emit("sync", "wait_only", waits=[("s_qw", P.v("s_qw")),
                                           ("s_xdump", P.v("s_xdump"))])
    elif upto == "mixeronly":
        weave([(phase0_gen(), 3), (mixer_gen(0), 1)])
        for c in range(1, nchunks):
            weave([(mixer_gen(c), 1)])
        P.emit("sync", "wait_only", waits=[("s_qw", P.v("s_qw")),
                                           ("s_xdump", P.v("s_xdump"))])
    elif upto == "p0only":
        weave([(phase0_gen(), 1)])
        P.emit("sync", "wait_only", waits=[("s_qw", P.v("s_qw"))])
    elif upto == "m0only":
        weave([(mixer_gen(0), 1)])
        P.emit("sync", "wait_only", waits=[("s_xdump", P.v("s_xdump"))])
    elif upto in ("f1g", "f1h"):
        weave([(phase0_gen(), 3), (mixer_gen(0), 1)])
        for hc in range(HB):
            slot = hc % 2
            wl = [("s_qw", QW_TOTAL)]
            if P.v("t_f1") >= 4:
                wl.append(("t_f1", P.v("t_f1") - 2))
            P.emit("sync", "dma", waits=wl, out=wq12[:, slot, :, :, :],
                   in_=q12s_f[hc], inc=("s_q12", 16))
            sq12 = P.v("s_q12")
            wmm = [("s_q12", sq12), ("v_v", vvE(0)), ("a_thr", 6),
                   ("a_f1", P.v("a_f1"))]
            for db in range(DC):
                P.emit("tensor", "mm", waits=(wmm if db == 0 else []),
                       out=pA[:, :], lhsT=wq12[:, slot, 0, db, :],
                       rhs=v_bf[:, db, :], start=(db == 0), stop=(db == DC - 1),
                       inc=(("t_f1", 1) if db == DC - 1 else None))
            for db in range(DC):
                P.emit("tensor", "mm", waits=[], out=pB[:, :],
                       lhsT=wq12[:, slot, 1, db, :], rhs=v_bf[:, db, :],
                       start=(db == 0), stop=(db == DC - 1),
                       inc=(("t_f1", 1) if db == DC - 1 else None))
            wg = [("t_f1", P.v("t_f1"))]
            if P.v("g_gate") >= 2:
                wg.append(("g_gate", P.v("g_gate") - 1))
            P.emit("scalar", "act", waits=wg,
                   out=xsq[:, hc % 2, :], in_=pA[:, :], func=AF.Sigmoid,
                   scale=half_ap, inc=("a_f1", 1))
            P.emit("scalar", "act", out=u_bf[:, hc % 2, :], in_=pB[:, :],
                   func=AF.Copy, scale=half_ap, inc=("a_f1", 1))
            if upto == "f1h":
                P.emit("gpsimd", "tt", waits=[("a_f1", P.v("a_f1"))],
                       out=hidden[:, hc, :], in0=xsq[:, hc % 2, :],
                       in1=u_bf[:, hc % 2, :], op=OP.mult, inc=("g_gate", 1))
        P.emit("sync", "wait_only", waits=[("a_f1", P.v("a_f1")),
                                           ("g_gate", P.v("g_gate")),
                                           ("s_xdump", P.v("s_xdump"))])
    elif upto == "f1m":
        weave([(phase0_gen(), 3), (mixer_gen(0), 1)])
        for hc in range(HB):
            slot = hc % 2
            wl = [("s_qw", QW_TOTAL)]
            if P.v("t_f1") >= 4:
                wl.append(("t_f1", P.v("t_f1") - 2))
            P.emit("sync", "dma", waits=wl, out=wq12[:, slot, :, :, :],
                   in_=q12s_f[hc], inc=("s_q12", 16))
            sq12 = P.v("s_q12")
            wmm = [("s_q12", sq12), ("v_v", vvE(0)), ("a_thr", 6)]
            for db in range(DC):
                P.emit("tensor", "mm", waits=(wmm if db == 0 else []),
                       out=pA[:, :], lhsT=wq12[:, slot, 0, db, :],
                       rhs=v_bf[:, db, :], start=(db == 0), stop=(db == DC - 1),
                       inc=(("t_f1", 1) if db == DC - 1 else None))
            for db in range(DC):
                P.emit("tensor", "mm", waits=[], out=pB[:, :],
                       lhsT=wq12[:, slot, 1, db, :], rhs=v_bf[:, db, :],
                       start=(db == 0), stop=(db == DC - 1),
                       inc=(("t_f1", 1) if db == DC - 1 else None))
        P.emit("sync", "wait_only", waits=[("t_f1", P.v("t_f1")),
                                           ("s_xdump", P.v("s_xdump"))])
    elif upto == "f1l":
        weave([(phase0_gen(), 3), (mixer_gen(0), 1)])
        for hc in range(HB):
            slot = hc % 2
            wl = [("s_qw", QW_TOTAL)]
            if P.v("t_f1") >= 4:
                wl.append(("t_f1", P.v("t_f1") - 2))
            P.emit("sync", "dma", waits=wl, out=wq12[:, slot, :, :, :],
                   in_=q12s_f[hc], inc=("s_q12", 16))
        P.emit("sync", "wait_only", waits=[("s_q12", P.v("s_q12")),
                                           ("s_xdump", P.v("s_xdump"))])
    elif upto == "f1":
        weave([(phase0_gen(), 3), (mixer_gen(0), 1)])
        fg = ffn_gen(0)
        for _ in range(HB):
            next(fg)
        P.emit("sync", "wait_only", waits=[("g_gate", P.v("g_gate")),
                                           ("s_xdump", P.v("s_xdump"))])
    if upto != "full":
        return _emit_all(nc, P)
    weave([(phase0_gen(), 3), (mixer_gen(0), 1)])
    for c in range(1, nchunks):
        fg = ffn_gen(c - 1)
        for _ in range(4):
            next(fg)
        weave([(mixer_gen(c), 1), (fg, 2)])
    weave([(ffn_gen(nchunks - 1), 1)])
    P.emit("sync", "wait_only", waits=[("s_out", P.v("s_out"))])
    return _emit_all(nc, P)


def _emit_all(nc, P):
    sem_names = sorted({nm for (_, _, ws, inc, _) in P.ops
                        for nm in ([inc[0]] if inc else []) + [s for s, _ in ws]})
    sems = {}
    from contextlib import ExitStack
    with ExitStack() as ctx:
        for nm in sem_names:
            sems[nm] = ctx.enter_context(nc.semaphore(nm))

        def emit_for(eng_obj, eng_name):
            waited = {}
            for (eng, kind, ws, inc, kw) in P.ops:
                if eng != eng_name:
                    continue
                for (s, v) in ws:
                    if waited.get(s, -1) >= v:
                        continue
                    eng_obj.wait_ge(sems[s], v)
                    waited[s] = v
                if kind == "dma":
                    ins = eng_obj.dma_start(out=kw["out"], in_=kw["in_"])
                elif kind == "mm":
                    ins = eng_obj.matmul(kw["out"], kw["lhsT"], kw["rhs"],
                                         start=kw["start"], stop=kw["stop"])
                elif kind == "act":
                    a = {}
                    for f in ("bias", "scale", "accum_out"):
                        if f in kw:
                            a[f] = kw[f]
                    ins = eng_obj.activation(out=kw["out"], in_=kw["in_"],
                                             func=kw["func"], **a)
                elif kind == "sign":
                    ins = eng_obj.sign(out=kw["out"], in_=kw["in_"],
                                       bias=kw["bias"])
                elif kind == "tt":
                    ins = eng_obj.tensor_tensor(out=kw["out"], in0=kw["in0"],
                                                in1=kw["in1"], op=kw["op"])
                elif kind == "ts":
                    ins = eng_obj.tensor_single_scalar(out=kw["out"],
                                                       in_=kw["in_"],
                                                       scalar=kw["scalar"],
                                                       op=kw["op"])
                elif kind == "stt":
                    ins = eng_obj.scalar_tensor_tensor(out=kw["out"],
                                                       in0=kw["in0"],
                                                       scalar=kw["scalar"],
                                                       in1=kw["in1"],
                                                       op0=kw["op0"],
                                                       op1=kw["op1"])
                elif kind == "scan":
                    ins = eng_obj.tensor_tensor_scan(out=kw["out"],
                                                     data0=kw["d0"],
                                                     data1=kw["d1"],
                                                     initial=kw["init"],
                                                     op0=kw["op0"],
                                                     op1=kw["op1"])
                elif kind == "copy":
                    ins = eng_obj.tensor_copy(out=kw["out"], in_=kw["in_"])
                elif kind == "memset":
                    ins = eng_obj.memset(kw["ap"], kw["val"])
                elif kind == "red":
                    ins = eng_obj.reduce_sum(out=kw["out"], in_=kw["in_"],
                                             axis=AX.X)
                elif kind == "recip":
                    ins = eng_obj.reciprocal(out=kw["out"], in_=kw["in_"])
                elif kind == "nop":
                    ins = eng_obj.nop()
                elif kind == "wait_only":
                    ins = None
                else:
                    raise ValueError(kind)
                if inc is not None and ins is not None:
                    ins.then_inc(sems[inc[0]], inc[1])

        with nc.Block() as blk:
            @blk.sync
            def _(e):
                emit_for(e, "sync")

            @blk.tensor
            def _(e):
                emit_for(e, "tensor")

            @blk.scalar
            def _(e):
                emit_for(e, "scalar")

            @blk.vector
            def _(e):
                emit_for(e, "vector")

            @blk.gpsimd
            def _(e):
                emit_for(e, "gpsimd")

    return nc


def host_layout(inputs):
    x = np.ascontiguousarray(np.asarray(inputs["L0_attn_x"], np.float32))
    W_xproj = np.asarray(inputs["L0_attn_W_xproj"], np.float32)
    W_dt = np.asarray(inputs["L0_attn_W_dt"], np.float32)
    b_dt = np.asarray(inputs["L0_attn_b_dt"], np.float32)
    A_log = np.asarray(inputs["L0_attn_A_log"], np.float32)
    D_param = np.asarray(inputs["L0_attn_D_param"], np.float32)
    w1 = np.asarray(inputs["L0_ffn_w1"], np.float32)
    w2 = np.asarray(inputs["L0_ffn_w2"], np.float32)
    w3 = np.asarray(inputs["L0_ffn_w3"], np.float32)

    shared = {
        "wxT": np.ascontiguousarray(W_xproj.T),
        "wdtT": np.ascontiguousarray(W_dt.T),
        "alogT": np.ascontiguousarray(
            A_log.reshape(DC, 128, N).transpose(1, 0, 2).reshape(128, DC * N)),
        "bdtT": np.ascontiguousarray(b_dt.reshape(DC, 128).T),
        "dparT": np.ascontiguousarray(D_param.reshape(DC, 128).T),
        "w1b": np.ascontiguousarray(w1.T.reshape(D, H2B, 256).transpose(1, 0, 2)),
        "w2b": np.ascontiguousarray(w2.T.reshape(D, H2B, 256).transpose(1, 0, 2)),
        "w3b": np.ascontiguousarray(w3.T.reshape(H, DC, 128).transpose(1, 0, 2)),
    }
    per_core = []
    for b in range(B):
        m = dict(shared)
        m["xT"] = np.ascontiguousarray(x[b].T)
        per_core.append(m)
    return per_core


def kernel(**inputs):
    in_maps = host_layout(inputs)
    nc = build_nc()
    res = run_bass_kernel_spmd(nc, in_maps, core_ids=list(range(B)))
    out = np.empty((B, L, D), np.float32)
    for b in range(B):
        out[b] = res.results[b]["outT"].T
    return out


# version 06
# speedup vs baseline: 1.0953x; 1.0953x over previous
import numpy as np
import ml_dtypes
import concourse.bass as bass
import concourse.mybir as mybir
from concourse.bass_utils import run_bass_kernel_spmd

F32 = mybir.dt.float32
BF16 = mybir.dt.bfloat16
AF = mybir.ActivationFunctionType
OP = mybir.AluOpType
AX = mybir.AxisListType
bf16 = ml_dtypes.bfloat16

B = 8
L = 2048
D = 2048
T = 512
NCH = L // T
DC = 16
N = 16
R = 64
E = R + 2 * N
H = 5632
HB = H // 128
H2B = H // 256
NW1 = H * D
NW3 = D * H
SIG_MIN = 1e-37


class Plan:
    __slots__ = ("ops", "sems")

    def __init__(self):
        self.ops = []
        self.sems = {}

    def v(self, name):
        return self.sems.get(name, 0)

    def emit(self, eng, kind, waits=(), inc=None, **kw):
        ws = [(s, val) for (s, val) in waits if val > 0]
        if inc is not None:
            self.sems[inc[0]] = self.sems.get(inc[0], 0) + inc[1]
        self.ops.append((eng, kind, tuple(ws), inc, kw))


def build_nc(nchunks=NCH, upto="full", noffn=False):
    nc = bass.Bass("TRN2", target_bir_lowering=False, debug=False,
                   dynamic_dma_scratch_size=2048)

    xT = nc.dram_tensor("xT", [D, L], F32, kind="ExternalInput")
    wxT = nc.dram_tensor("wxT", [D, E], F32, kind="ExternalInput")
    wdtT = nc.dram_tensor("wdtT", [R, D], F32, kind="ExternalInput")
    alogT = nc.dram_tensor("alogT", [128, DC * N], F32, kind="ExternalInput")
    bdtT = nc.dram_tensor("bdtT", [128, DC], F32, kind="ExternalInput")
    dparT = nc.dram_tensor("dparT", [128, DC], F32, kind="ExternalInput")
    w1b = nc.dram_tensor("w1b", [D, H], F32, kind="ExternalInput")
    w2b = nc.dram_tensor("w2b", [D, H], F32, kind="ExternalInput")
    w3b = nc.dram_tensor("w3b", [H, D], F32, kind="ExternalInput")
    outT = nc.dram_tensor("outT", [D, L], F32, kind="ExternalOutput")
    q12s = nc.dram_tensor("q12s", [HB, 128, 2, 16, 128], BF16, kind="Internal")
    q3s = nc.dram_tensor("q3s", [DC, 128, HB, 128], BF16, kind="Internal")
    xscr = nc.dram_tensor("xscr", [2, 128, DC, T], F32, kind="Internal")

    xT_v = xT.rearrange("(db p) t -> p db t", p=128)
    outT_v = outT.rearrange("(db p) t -> p db t", p=128)
    w1b_v = w1b.rearrange("(db p) (x e) -> db x p e", p=128, x=2)
    w2b_v = w2b.rearrange("(db p) (x e) -> db x p e", p=128, x=2)
    w3b_v = w3b.rearrange("(hb p) d -> hb p d", p=128)
    q12s_q = q12s.rearrange("(x k) p m db h -> x m db p k h", x=2)
    q3s_q = q3s.rearrange("b p hb d -> hb p b d")
    q12s_f = q12s
    q3s_f = q3s
    wxT_v = wxT.rearrange("(db p) e -> p db e", p=128)
    wdtT_v = wdtT.rearrange("r (db p) -> r db p", p=128)

    sb = nc.alloc_sbuf_tensor
    x_res = sb("x_res", [128, DC, T], F32)
    BCbig = sb("BCbig", [128, 32, T], BF16)
    hv = sb("hv", [128, 16, T], BF16)
    v_bf = sb("v_bf", [128, DC, T], BF16)
    hidden = sb("hidden", [128, HB, T], BF16)
    warena = sb("warena", [128, 12288], BF16)
    dAr = sb("dAr", [128, 2, T], F32)
    dBxr = sb("dBxr", [128, 2, T], BF16)
    spx = sb("spx", [128, 2, T], F32)
    sigb = sb("sigb", [128, T], F32)
    yt = sb("yt", [128, T], F32)
    u2 = sb("u2", [128, T], F32)
    u_bf = sb("u_bf", [128, 2, T], BF16)
    xsq = sb("xsq", [128, 2, T], BF16)
    draw = sb("draw", [128, T], BF16)
    bcneg = sb("bcneg", [32, T], BF16)
    bcsp = sb("bcsp", [128, 11, T], BF16)
    mrep = sb("mrep", [128, T], F32)
    rrep = sb("rrep", [128, T], F32)
    rows = sb("rows", [1, 2, T], F32)
    Wx = sb("Wx", [128, DC, E], BF16)
    Wdt = sb("Wdt", [128, 8 * 128], BF16)
    negA = sb("negA", [128, DC * N], F32)
    negb = sb("negb", [128, DC], F32)
    dpar = sb("dpar", [128, DC], F32)
    hstate = sb("hstate", [128, DC, 2 * N], F32)
    outst = sb("outst", [128, 2, T], F32)
    xf2 = sb("xf2", [128, T], F32)
    gcol = sb("gcol", [128, 46], F32)
    thr = sb("thr", [128, 8], F32)
    gtot = sb("gtot", [1, 8], F32)
    consts = sb("consts", [128, 4], F32)
    constsb = sb("constsb", [128, 2], BF16)
    onesr = sb("onesr", [1, 128], F32)
    onesrb = sb("onesrb", [65, 128], BF16)

    Wdt_v = Wdt.reshape([128, 8, 128])
    hidf = hidden.reshape([128, HB * T]).bitcast(F32)
    hidb = hidden.reshape([128, HB * T])
    qflat = warena[:, 0:4096]
    junk = warena[:, 4096:8192]
    sgn1 = warena[:, 8192:12288]
    sgn2 = hidb[:, 16384:20480]
    warf = warena.bitcast(F32)
    wq12 = warena[:, 0:8192].rearrange("p (s m db h) -> p s m db h",
                                       s=2, m=2, db=16)
    wq3 = warena
    gat4 = outst.bitcast(BF16).reshape([128, 4, T])

    ps = nc.alloc_psum_tensor
    pA = ps("pA", [128, T], F32)
    pB = ps("pB", [128, T], F32)
    pC = ps("pC", [128, T], F32)
    pD = ps("pD", [128, T], F32)
    pE = ps("pE", [128, T], F32)
    pF = ps("pF", [128, T], F32)
    pG = ps("pG", [128, T], F32)
    pH = ps("pH", [128, T], F32)
    pdel = [pG, pH]
    prep = [pE, pF]
    pf2 = [pC, pD]

    P = Plan()
    ones_f = consts[:, 0:1]
    ones_b = constsb[:, 0:1]
    eps_ap = consts[0:1, 1:2]
    half_ap = consts[:, 2:3]

    tf1E = lambda c: max(0, 88 * (c + 1))
    tf2E = lambda c: max(0, 16 * (c + 1))
    ggE = lambda c: max(0, 44 * (c + 1))
    soutE = lambda c: max(0, 256 * (c + 1))
    vvE = lambda c: max(0, 16 * (c + 1))
    vresE = lambda c: max(0, 16 * (c + 1))
    gdbxE = lambda c: max(0, 256 * (c + 1))
    sxdE = lambda c: max(0, 16 * (c + 1))

    def init_ops():
        P.emit("sync", "dma", out=dAr[:, 0, 0:DC * N], in_=alogT.ap(),
               inc=("s_misc", 16))
        P.emit("sync", "dma", out=negb[:, :], in_=bdtT.ap(), inc=("s_misc", 16))
        P.emit("sync", "dma", out=dpar[:, :], in_=dparT.ap(), inc=("s_misc", 16))
        P.emit("sync", "dma", out=warf[:, 0:1536]
               .rearrange("p (db e) -> p db e", db=16), in_=wxT_v,
               inc=("s_cast", 16))
        P.emit("sync", "dma", out=warf[0:64, 2048:3072]
               .rearrange("p (db k) -> p db k", db=8), in_=wdtT_v[:, 0:8, :],
               inc=("s_cast", 16))
        P.emit("sync", "dma", out=warf[64:128, 2048:3072]
               .rearrange("p (db k) -> p db k", db=8), in_=wdtT_v[:, 8:16, :],
               inc=("s_cast", 16))
        P.emit("scalar", "act", waits=[("s_cast", 16)],
               out=Wx.reshape([128, DC * E]).ap(), in_=warf[:, 0:1536],
               func=AF.Copy, inc=("a_cast", 1))
        P.emit("scalar", "act", waits=[("s_cast", 48)], out=Wdt[:, :],
               in_=warf[:, 2048:3072], func=AF.Copy, inc=("a_cast", 1))
        P.emit("vector", "memset", ap=consts[:, 0:1], val=1.0)
        P.emit("vector", "memset", ap=consts[:, 1:2], val=1e-5)
        P.emit("vector", "memset", ap=consts[:, 2:3], val=0.5)
        P.emit("vector", "memset", ap=constsb[:, 0:1], val=1.0)
        P.emit("vector", "memset", ap=onesr[:, :], val=1.0)
        P.emit("vector", "memset", ap=onesrb[:, :], val=1.0)
        P.emit("vector", "memset", ap=hstate[:, :, :], val=0.0)
        P.emit("scalar", "act", waits=[("s_misc", 16)], out=negA[:, :],
               in_=dAr[:, 0, 0:DC * N], func=AF.Exp, inc=("a_init", 1))
        P.emit("vector", "ts", waits=[("s_misc", 48), ("a_init", 1)],
               out=negA[:, :], in_=negA[:, :], scalar=-1.0, op=OP.mult,
               inc=("v_init", 1))
        P.emit("sync", "dma", out=x_res[:, :, :], in_=xT_v[:, :, 0:T],
               inc=("s_x", 16))

    def phase0_gen():
        mats = [
            ("w1", [w1b_v[db, x] for db in range(DC) for x in range(2)],
             [q12s_q[x, 0, db] for db in range(DC) for x in range(2)], 2816, NW1),
            ("w2", [w2b_v[db, x] for db in range(DC) for x in range(2)],
             [q12s_q[x, 1, db] for db in range(DC) for x in range(2)], 2816, NW1),
            ("w3", [w3b_v[hb] for hb in range(HB)],
             [q3s_q[hb] for hb in range(HB)], 2048, NW3),
        ]
        slabf = hidf
        for mi, (name, wv, qv, EL, numel) in enumerate(mats):
            S = len(wv)
            ga = "a_gacc_" + name
            for s in range(S):
                o = (s % 2) * 4096
                w = []
                if s >= 2:
                    w.append((ga, s - 1))
                elif mi > 0:
                    w.append(("a_q", P.v("a_q")))
                P.emit("sync", "dma", waits=w, out=slabf[:, o:o + EL],
                       in_=wv[s], inc=("s_w", 16))
                aw = [("s_w", P.v("s_w"))]
                if s == 0 and mi > 0:
                    aw.append(("v_gred", mi))
                P.emit("scalar", "act", waits=aw, out=junk[:, 0:EL],
                       in_=slabf[:, o:o + EL], func=AF.Abs,
                       accum_out=gcol[:, s:s + 1], inc=(ga, 1))
                yield
            P.emit("vector", "red", waits=[(ga, S)], out=gcol[:, 44:45],
                   in_=gcol[:, 0:S], inc=("v_gred", 1))
            P.emit("tensor", "mm", waits=[("v_gred", P.v("v_gred")),
                                          ("a_thr", P.v("a_thr"))],
                   out=pA[0:1, 0:1], lhsT=ones_f, rhs=gcol[:, 44:45],
                   start=True, stop=True, inc=("t_thr", 1))
            P.emit("vector", "ts", waits=[("t_thr", P.v("t_thr"))],
                   out=gtot[0:1, mi:mi + 1], in_=pA[0:1, 0:1],
                   scalar=float(1.0 / (2.0 * numel)), op=OP.mult,
                   inc=("v_gtot", 1))
            P.emit("tensor", "mm", waits=[("v_gtot", P.v("v_gtot"))],
                   out=pB[:, 0:1], lhsT=onesr[:, :], rhs=gtot[0:1, mi:mi + 1],
                   start=True, stop=True, inc=("t_thr", 1))
            P.emit("scalar", "act", waits=[("t_thr", P.v("t_thr"))],
                   out=thr[:, 2 * mi:2 * mi + 1], in_=pB[:, 0:1], func=AF.Copy,
                   scale=-1.0, inc=("a_thr", 1))
            P.emit("scalar", "act", out=thr[:, 2 * mi + 1:2 * mi + 2],
                   in_=pB[:, 0:1], func=AF.Copy, inc=("a_thr", 1))
            athr = P.v("a_thr")
            for s in range(S):
                o = (s % 2) * 4096
                w = [("a_q", P.v("a_q") - 2)] if s >= 2 else [(ga, S)]
                P.emit("sync", "dma", waits=w, out=slabf[:, o:o + EL],
                       in_=wv[s], inc=("s_w", 16))
                P.emit("scalar", "sign",
                       waits=[("s_w", P.v("s_w")), ("a_thr", athr)],
                       out=sgn1[:, 0:EL], in_=slabf[:, o:o + EL],
                       bias=thr[:, 2 * mi:2 * mi + 1], inc=("a_q", 1))
                P.emit("scalar", "sign", out=sgn2[:, 0:EL],
                       in_=slabf[:, o:o + EL],
                       bias=thr[:, 2 * mi + 1:2 * mi + 2], inc=("a_q", 1))
                wq = [("a_q", P.v("a_q")), ("s_qw", P.v("s_qw"))]
                if P.v("v_q") == 0:
                    wq.append(("a_cast", 2))
                P.emit("vector", "tt", waits=wq, out=qflat[:, 0:EL],
                       in0=sgn1[:, 0:EL], in1=sgn2[:, 0:EL], op=OP.add,
                       inc=("v_q", 1))
                nk = EL // 128
                qfk = qflat[:, 0:EL].rearrange("p (k h) -> p k h", h=128)
                P.emit("scalar", "dma", waits=[("v_q", P.v("v_q"))],
                       out=qv[s][:, 0:nk // 2, :], in_=qfk[:, 0:nk // 2, :],
                       inc=("s_qw", 16))
                P.emit("scalar", "dma",
                       out=qv[s][:, nk // 2:nk, :], in_=qfk[:, nk // 2:nk, :],
                       inc=("s_qw", 16))
                yield

    QW_TOTAL = 16 * 2 * (2 * DC * 2 + HB)

    def mixer_gen(c):
        sx = 16 * (c + 1)
        for ln in (0, 1):
            vr = vresE(c) if ln == 1 else 0
            vo = P.v("v_row")
            for dc in range(DC):
                wq = [("s_x", sx), ("v_init", 1)]
                if ln == 1:
                    wq.append(("v_resid", vr))
                wsq = list(wq)
                if P.v("t_statq") >= 2:
                    wsq.append(("t_statq", P.v("t_statq") - 1))
                P.emit("scalar", "act", waits=wsq, out=xsq[:, dc % 2, :],
                       in_=x_res[:, dc, :], func=AF.Square, inc=("a_sq", 1))
                P.emit("tensor", "mm", waits=wq + [("v_row", vo),
                                                   ("a_lnb", P.v("a_lnb")),
                                                   ("a_rep", P.v("a_rep"))],
                       out=pE[0:1, :], lhsT=ones_f, rhs=x_res[:, dc, :],
                       start=(dc == 0), stop=(dc == DC - 1),
                       inc=(("t_stat", 1) if dc == DC - 1 else None))
                P.emit("tensor", "mm", waits=[("a_sq", P.v("a_sq"))],
                       out=pF[0:1, :], lhsT=ones_b, rhs=xsq[:, dc % 2, :],
                       start=(dc == 0), stop=(dc == DC - 1),
                       inc=("t_statq", 1))
                if dc == DC - 1:
                    P.emit("tensor", "nop", inc=("t_stat", 1))
                if dc % 4 == 3:
                    yield
            ts_ = P.v("t_stat")
            P.emit("vector", "ts", waits=[("t_stat", ts_),
                                          ("t_statq", P.v("t_statq")),
                                          ("t_bc", P.v("t_bc"))],
                   out=rows[0:1, 0, :], in_=pE[0:1, :], scalar=1.0 / D,
                   op=OP.mult)
            P.emit("vector", "tt", out=rows[0:1, 1, :], in0=rows[0:1, 0, :],
                   in1=rows[0:1, 0, :], op=OP.mult)
            P.emit("vector", "stt", out=rows[0:1, 1, :], in0=pF[0:1, :],
                   scalar=1.0 / D, in1=rows[0:1, 1, :], op0=OP.mult,
                   op1=OP.subtract, inc=("v_row", 1))
            P.emit("scalar", "act", waits=[("v_row", P.v("v_row"))],
                   out=pF[0:1, :], in_=rows[0:1, 1, :], func=AF.Sqrt,
                   bias=eps_ap, inc=("a_sqrt", 1))
            P.emit("vector", "recip", waits=[("a_sqrt", P.v("a_sqrt"))],
                   out=rows[0:1, 1, :], in_=pF[0:1, :], inc=("v_row", 1))
            P.emit("tensor", "mm", waits=[("v_row", P.v("v_row"))],
                   out=pE[:, :], lhsT=onesr[:, :], rhs=rows[0:1, 0, :],
                   start=True, stop=True, inc=("t_bc", 1))
            P.emit("tensor", "mm", out=pF[:, :], lhsT=onesr[:, :],
                   rhs=rows[0:1, 1, :], start=True, stop=True, inc=("t_bc", 1))
            P.emit("scalar", "act", waits=[("t_bc", P.v("t_bc"))],
                   out=mrep[:, :], in_=pE[:, :], func=AF.Copy, inc=("a_lnb", 1))
            P.emit("scalar", "act", out=rrep[:, :], in_=pF[:, :], func=AF.Copy,
                   inc=("a_lnb", 1))
            alnb = P.v("a_lnb")
            yield

            if ln == 0:
                for dc in range(DC):
                    wu = [("a_lnb", alnb)]
                    if dc == 0:
                        wu.append(("g_dbx", gdbxE(c - 1)))
                    if P.v("t_xdblq") >= 2:
                        wu.append(("t_xdblq", P.v("t_xdblq") - 1))
                    P.emit("vector", "tt", waits=wu, out=u2[:, :],
                           in0=x_res[:, dc, :], in1=mrep[:, :], op=OP.subtract)
                    P.emit("vector", "tt", out=u_bf[:, dc % 2, :], in0=u2[:, :],
                           in1=rrep[:, :], op=OP.mult, inc=("v_u", 1))
                    wm = [("v_u", P.v("v_u")), ("a_cast", 2)]
                    if dc == 0:
                        wm.append(("a_lnb", alnb))
                    P.emit("tensor", "mm", waits=wm, out=pE[0:64, :],
                           lhsT=Wx[:, dc, 0:64], rhs=u_bf[:, dc % 2, :],
                           start=(dc == 0), stop=(dc == DC - 1),
                           inc=(("t_xdbl", 1) if dc == DC - 1 else None))
                    P.emit("tensor", "mm", waits=[], out=pF[0:32, :],
                           lhsT=Wx[:, dc, 64:96], rhs=u_bf[:, dc % 2, :],
                           start=(dc == 0), stop=(dc == DC - 1),
                           inc=("t_xdblq", 1))
                    if dc == DC - 1:
                        P.emit("tensor", "nop", inc=("t_xdbl", 1))
                    if dc % 4 == 3:
                        yield
                txd = P.v("t_xdbl")
                P.emit("scalar", "act",
                       waits=[("t_xdbl", txd), ("t_xdblq", P.v("t_xdblq")),
                              ("t_del", P.v("t_del"))],
                       out=draw[0:64, :], in_=pE[0:64, :], func=AF.Copy,
                       inc=("a_evac", 1))
                P.emit("sync", "dma", waits=[("a_evac", P.v("a_evac")),
                                             ("t_del", P.v("t_del"))],
                       out=draw[64:128, :], in_=draw[0:64, :],
                       inc=("s_dr", 16))
                P.emit("scalar", "act", waits=[("s_bc", P.v("s_bc"))],
                       out=bcneg[0:32, :], in_=pF[0:32, :], func=AF.Copy,
                       inc=("a_evac", 1))
                aev = P.v("a_evac")
                P.emit("sync", "dma", waits=[("a_evac", aev),
                                             ("t_rep", P.v("t_rep"))],
                       out=bcsp[0:1, 0:11, :], in_=bcneg[0:11, :],
                       inc=("s_bc", 16))
                P.emit("sync", "dma", out=bcsp[32:33, 0:11, :],
                       in_=bcneg[11:22, :], inc=("s_bc", 16))
                P.emit("sync", "dma", out=bcsp[64:65, 0:10, :],
                       in_=bcneg[22:32, :], inc=("s_bc", 16))
                sbc = P.v("s_bc")
                yield
                for j in range(32):
                    base, slt = (32 * (j // 11), j % 11) if j < 22 else (64, j - 22)
                    wr = [("s_bc", sbc)]
                    if j >= 2:
                        wr.append(("a_rep", P.v("a_rep") - 1))
                    else:
                        wr.append(("a_evac", aev))
                    P.emit("tensor", "mm", waits=wr, out=prep[j % 2][:, :],
                           lhsT=onesrb[base:base + 1, :],
                           rhs=bcsp[base:base + 1, slt, :],
                           start=True, stop=True, inc=("t_rep", 1))
                    wbc = [("t_rep", P.v("t_rep"))]
                    if j == 0:
                        wbc.append(("g_dbx", gdbxE(c - 1)))
                        wbc.append(("v_resid", vresE(c - 1)))
                    P.emit("scalar", "act", waits=wbc, out=BCbig[:, j, :],
                           in_=prep[j % 2][:, :], func=AF.Copy, inc=("a_rep", 1))
                    if j % 8 == 7:
                        yield
                arep = P.v("a_rep")

                for dc in range(DC):
                    pd = pdel[dc % 2]
                    wd = [("a_evac", aev)]
                    if P.v("t_del") >= 2:
                        wd.append(("a_pd", P.v("a_pd") - 1))
                    if dc < 8:
                        dsl, drs = Wdt_v[0:64, dc, :], draw[0:64, :]
                    else:
                        dsl, drs = Wdt_v[64:128, dc - 8, :], draw[64:128, :]
                        wd.append(("s_dr", 16 * (c + 1)))
                    P.emit("tensor", "mm", waits=wd, out=pd[:, :],
                           lhsT=dsl, rhs=drs,
                           start=True, stop=True, inc=("t_del", 1))
                    P.emit("scalar", "act",
                           waits=[("t_del", P.v("t_del")), ("v_init", 1),
                                  ("v_sx", max(0, P.v("v_sx") - 1))],
                           out=sigb[:, :], in_=pd[:, :], func=AF.Relu,
                           bias=negb[:, dc:dc + 1], inc=("a_r", 1))
                    P.emit("vector", "stt", waits=[("a_r", P.v("a_r"))],
                           out=pd[:, :], in0=sigb[:, :], scalar=-2.0,
                           in1=pd[:, :], op0=OP.mult, op1=OP.add,
                           inc=("v_sx", 1))
                    we = [("v_sx", P.v("v_sx")),
                          ("v_du", max(0, P.v("v_du") - 1))]
                    P.emit("scalar", "act", waits=we, out=spx[:, dc % 2, :],
                           in_=pd[:, :], func=AF.Exp,
                           bias=negb[:, dc:dc + 1], inc=("a_pd", 1))
                    P.emit("scalar", "act", out=spx[:, dc % 2, :],
                           in_=spx[:, dc % 2, :], func=AF.Ln,
                           bias=ones_f, inc=("a_lp", 1))
                    P.emit("vector", "tt", waits=[("a_lp", P.v("a_lp"))],
                           out=spx[:, dc % 2, :], in0=sigb[:, :],
                           in1=spx[:, dc % 2, :], op=OP.add, inc=("v_sp", 1))
                    vsp = P.v("v_sp")
                    wdu = []
                    if dc < 2:
                        wdu.append(("t_xdbl", txd))
                        wdu.append(("t_xdblq", 16 * (c + 1)))
                    else:
                        wdu.append(("g_dbx", gdbxE(c - 1) + 16 * (dc - 1)))
                    P.emit("vector", "tt", out=u2[:, :], in0=x_res[:, dc, :],
                           in1=mrep[:, :], op=OP.subtract)
                    P.emit("vector", "tt", out=u2[:, :], in0=u2[:, :],
                           in1=rrep[:, :], op=OP.mult)
                    P.emit("vector", "tt", waits=wdu, out=u_bf[:, dc % 2, :],
                           in0=spx[:, dc % 2, :], in1=u2[:, :], op=OP.mult,
                           inc=("v_du", 1))
                    for n in range(N):
                        k = dc * N + n
                        gi = c * DC * N + k
                        wdA = [("v_sp", vsp)]
                        if gi >= 2:
                            wdA.append(("v_scan", P.v("v_scan") - 1))
                        P.emit("scalar", "act", waits=wdA, out=dAr[:, n % 2, :],
                               in_=spx[:, dc % 2, :], func=AF.Exp,
                               scale=negA[:, k:k + 1], inc=("a_dA", 1))
                        wbx = [("a_rep", arep)] if (dc == 0 and n == 0) else []
                        P.emit("vector", "tt", waits=wbx,
                               out=dBxr[:, n % 2, :], in0=u_bf[:, dc % 2, :],
                               in1=BCbig[:, n, :], op=OP.mult, inc=("g_dbx", 1))
                        wsc = [("a_dA", P.v("a_dA"))]
                        if c == 0 and dc == 0 and n == 0:
                            wsc.append(("v_init", 1))
                        P.emit("vector", "scan", waits=wsc, out=hv[:, n, :],
                               d0=dAr[:, n % 2, :], d1=dBxr[:, n % 2, :],
                               init=hstate[:, dc, 2 * n + 1:2 * n + 2],
                               op0=OP.mult, op1=OP.add, inc=("v_scan", 1))
                    for n in range(N):
                        P.emit("vector", "copy",
                               out=hstate[:, dc, 2 * n:2 * n + 2],
                               in_=hv[:, n, T - 2:T])
                    P.emit("vector", "tt", out=hv[:, 0:16, :],
                           in0=hv[:, 0:16, :], in1=BCbig[:, 16:32, :],
                           op=OP.mult)
                    P.emit("vector", "tt", out=hv[:, 0:8, :], in0=hv[:, 0:8, :],
                           in1=hv[:, 8:16, :], op=OP.add)
                    P.emit("vector", "tt", out=hv[:, 0:4, :], in0=hv[:, 0:4, :],
                           in1=hv[:, 4:8, :], op=OP.add)
                    P.emit("vector", "tt", out=hv[:, 0:2, :], in0=hv[:, 0:2, :],
                           in1=hv[:, 2:4, :], op=OP.add)
                    P.emit("vector", "tt", out=yt[:, :], in0=hv[:, 0, :],
                           in1=hv[:, 1, :], op=OP.add)
                    P.emit("vector", "stt", out=u2[:, :], in0=u2[:, :],
                           scalar=dpar[:, dc:dc + 1], in1=yt[:, :],
                           op0=OP.mult, op1=OP.add)
                    P.emit("vector", "tt", out=x_res[:, dc, :],
                           in0=x_res[:, dc, :], in1=u2[:, :], op=OP.add,
                           inc=("v_resid", 1))
                    yield
            else:
                for dc in range(DC):
                    wv = [("a_lnb", alnb)]
                    if dc == 0 and not noffn:
                        wv.append(("t_f1", tf1E(c - 1)))
                    P.emit("vector", "tt", waits=wv, out=u2[:, :],
                           in0=x_res[:, dc, :], in1=mrep[:, :], op=OP.subtract)
                    P.emit("vector", "tt", out=v_bf[:, dc, :], in0=u2[:, :],
                           in1=rrep[:, :], op=OP.mult, inc=("v_v", 1))
                    if dc % 4 == 3:
                        yield
        P.emit("sync", "dma", waits=[("v_v", P.v("v_v"))], out=xscr[c % 2],
               in_=x_res[:, :, :], inc=("s_xdump", 16))
        if c + 1 < nchunks:
            P.emit("sync", "dma", waits=[("s_xdump", P.v("s_xdump"))],
                   out=x_res[:, :, :], in_=xT_v[:, :, (c + 1) * T:(c + 2) * T],
                   inc=("s_x", 16))
        yield

    def ffn_gen(c):
        pend = []
        for hc in range(HB):
            slot = hc % 2
            wl = [("t_f2", tf2E(c - 1))]
            if c == 0:
                wl.append(("s_qw", QW_TOTAL))
            if P.v("t_f1") - tf1E(c - 1) >= 4:
                wl.append(("t_f1", P.v("t_f1") - 2))
            P.emit("sync", "dma", waits=wl, out=wq12[:, slot, :, :, :],
                   in_=q12s_f[hc], inc=("s_q12", 16))
            sq12 = P.v("s_q12")
            wmm = [("s_q12", sq12), ("v_v", vvE(c)),
                   ("a_f1", P.v("a_f1"))]
            if hc < 2 and c == 0:
                wmm.append(("a_thr", 6))
            for db in range(DC):
                P.emit("tensor", "mm", waits=(wmm if db == 0 else []),
                       out=pA[:, :], lhsT=wq12[:, slot, 0, db, :],
                       rhs=v_bf[:, db, :], start=(db == 0), stop=(db == DC - 1),
                       inc=(("t_f1", 1) if db == DC - 1 else None))
            for db in range(DC):
                P.emit("tensor", "mm", waits=[], out=pB[:, :],
                       lhsT=wq12[:, slot, 1, db, :], rhs=v_bf[:, db, :],
                       start=(db == 0), stop=(db == DC - 1),
                       inc=(("t_f1", 1) if db == DC - 1 else None))
            wg = [("t_f1", P.v("t_f1"))]
            if hc == 0:
                wg.append(("t_f2", tf2E(c - 1)))
            if hc < 4:
                wg.append(("s_out", soutE(c - 1)))
            else:
                wg.append(("g_gate", ggE(c - 1) + hc - 3))
            P.emit("scalar", "act", waits=wg, out=hidden[:, hc, :],
                   in_=pA[:, :], func=AF.Copy, scale=half_ap, inc=("a_f1", 1))
            P.emit("scalar", "act", out=gat4[:, hc % 4, :], in_=pB[:, :],
                   func=AF.Copy, scale=half_ap, inc=("a_f1", 1))
            pend.append(hc)
            if len(pend) == 4 or hc == HB - 1:
                for h2 in pend:
                    P.emit("scalar", "act", out=hidden[:, h2, :],
                           in_=hidden[:, h2, :], func=AF.Sigmoid,
                           inc=("a_sg", 1))
                for h2 in pend:
                    P.emit("gpsimd", "tt", waits=[("a_sg", P.v("a_sg"))],
                           out=hidden[:, h2, :], in0=hidden[:, h2, :],
                           in1=gat4[:, h2 % 4, :], op=OP.mult,
                           inc=("g_gate", 1))
                pend.clear()
            yield
        for db in range(DC):
            slot = db % 2
            wl = [("t_f1", tf1E(c))]
            if c == 0:
                wl.append(("s_qw", QW_TOTAL))
            if P.v("t_f2") - tf2E(c - 1) >= 2:
                wl.append(("t_f2", P.v("t_f2") - 1))
            q3view = wq3[:, slot * 5632:slot * 5632 + 5632].rearrange(
                "p (hb d) -> p hb d", hb=HB)
            P.emit("sync", "dma", waits=wl, out=q3view, in_=q3s_f[db],
                   inc=("s_q3", 16))
            sq3 = P.v("s_q3")
            bk = pf2[db % 2]
            wmm = [("s_q3", sq3), ("g_gate", ggE(c))]
            if db >= 2:
                wmm.append(("a_o1", P.v("a_o1") - 1))
            for hb in range(HB):
                P.emit("tensor", "mm", waits=(wmm if hb == 0 else []),
                       out=bk[:, :], lhsT=q3view[:, hb, :],
                       rhs=hidden[:, hb, :], start=(hb == 0),
                       stop=(hb == HB - 1),
                       inc=(("t_f2", 1) if hb == HB - 1 else None))
            wx2 = [("s_xdump", sxdE(c)), ("g_out", P.v("g_out"))]
            P.emit("sync", "dma", waits=wx2, out=xf2[:, :],
                   in_=xscr[c % 2][:, db, :], inc=("s_xf2", 16))
            wo1 = [("t_f2", P.v("t_f2")), ("g_gate", ggE(c)),
                   ("s_out", 16 * (16 * c + db - 1))]
            P.emit("scalar", "act", waits=wo1, out=outst[:, db % 2, :],
                   in_=bk[:, :], func=AF.Copy, scale=half_ap, inc=("a_o1", 1))
            P.emit("gpsimd", "tt", waits=[("a_o1", P.v("a_o1")),
                                          ("s_xf2", P.v("s_xf2"))],
                   out=outst[:, db % 2, :], in0=outst[:, db % 2, :],
                   in1=xf2[:, :], op=OP.add, inc=("g_out", 1))
            P.emit("sync", "dma", waits=[("g_out", P.v("g_out"))],
                   out=outT_v[:, db, c * T:(c + 1) * T],
                   in_=outst[:, db % 2, :], inc=("s_out", 16))
            yield

    def weave(gens_ratios):
        active = [[g, w] for (g, w) in gens_ratios]
        while active:
            for item in list(active):
                g, w = item
                for _ in range(w):
                    try:
                        next(g)
                    except StopIteration:
                        active.remove(item)
                        break

    init_ops()
    if upto == "phase0":
        weave([(phase0_gen(), 1)])
        P.emit("sync", "wait_only", waits=[("s_qw", P.v("s_qw"))])
    elif upto == "mixer":
        weave([(phase0_gen(), 3), (mixer_gen(0), 1)])
        P.emit("sync", "wait_only", waits=[("s_qw", P.v("s_qw")),
                                           ("s_xdump", P.v("s_xdump"))])
    elif upto == "mixeronly":
        weave([(phase0_gen(), 3), (mixer_gen(0), 1)])
        for c in range(1, nchunks):
            weave([(mixer_gen(c), 1)])
        P.emit("sync", "wait_only", waits=[("s_qw", P.v("s_qw")),
                                           ("s_xdump", P.v("s_xdump"))])
    elif upto == "p0only":
        weave([(phase0_gen(), 1)])
        P.emit("sync", "wait_only", waits=[("s_qw", P.v("s_qw"))])
    elif upto == "m0only":
        weave([(mixer_gen(0), 1)])
        P.emit("sync", "wait_only", waits=[("s_xdump", P.v("s_xdump"))])
    elif upto in ("f1g", "f1h"):
        weave([(phase0_gen(), 3), (mixer_gen(0), 1)])
        for hc in range(HB):
            slot = hc % 2
            wl = [("s_qw", QW_TOTAL)]
            if P.v("t_f1") >= 4:
                wl.append(("t_f1", P.v("t_f1") - 2))
            P.emit("sync", "dma", waits=wl, out=wq12[:, slot, :, :, :],
                   in_=q12s_f[hc], inc=("s_q12", 16))
            sq12 = P.v("s_q12")
            wmm = [("s_q12", sq12), ("v_v", vvE(0)), ("a_thr", 6),
                   ("a_f1", P.v("a_f1"))]
            for db in range(DC):
                P.emit("tensor", "mm", waits=(wmm if db == 0 else []),
                       out=pA[:, :], lhsT=wq12[:, slot, 0, db, :],
                       rhs=v_bf[:, db, :], start=(db == 0), stop=(db == DC - 1),
                       inc=(("t_f1", 1) if db == DC - 1 else None))
            for db in range(DC):
                P.emit("tensor", "mm", waits=[], out=pB[:, :],
                       lhsT=wq12[:, slot, 1, db, :], rhs=v_bf[:, db, :],
                       start=(db == 0), stop=(db == DC - 1),
                       inc=(("t_f1", 1) if db == DC - 1 else None))
            wg = [("t_f1", P.v("t_f1"))]
            if P.v("g_gate") >= 2:
                wg.append(("g_gate", P.v("g_gate") - 1))
            P.emit("scalar", "act", waits=wg,
                   out=xsq[:, hc % 2, :], in_=pA[:, :], func=AF.Sigmoid,
                   scale=half_ap, inc=("a_f1", 1))
            P.emit("scalar", "act", out=u_bf[:, hc % 2, :], in_=pB[:, :],
                   func=AF.Copy, scale=half_ap, inc=("a_f1", 1))
            if upto == "f1h":
                P.emit("gpsimd", "tt", waits=[("a_f1", P.v("a_f1"))],
                       out=hidden[:, hc, :], in0=xsq[:, hc % 2, :],
                       in1=u_bf[:, hc % 2, :], op=OP.mult, inc=("g_gate", 1))
        P.emit("sync", "wait_only", waits=[("a_f1", P.v("a_f1")),
                                           ("g_gate", P.v("g_gate")),
                                           ("s_xdump", P.v("s_xdump"))])
    elif upto == "f1m":
        weave([(phase0_gen(), 3), (mixer_gen(0), 1)])
        for hc in range(HB):
            slot = hc % 2
            wl = [("s_qw", QW_TOTAL)]
            if P.v("t_f1") >= 4:
                wl.append(("t_f1", P.v("t_f1") - 2))
            P.emit("sync", "dma", waits=wl, out=wq12[:, slot, :, :, :],
                   in_=q12s_f[hc], inc=("s_q12", 16))
            sq12 = P.v("s_q12")
            wmm = [("s_q12", sq12), ("v_v", vvE(0)), ("a_thr", 6)]
            for db in range(DC):
                P.emit("tensor", "mm", waits=(wmm if db == 0 else []),
                       out=pA[:, :], lhsT=wq12[:, slot, 0, db, :],
                       rhs=v_bf[:, db, :], start=(db == 0), stop=(db == DC - 1),
                       inc=(("t_f1", 1) if db == DC - 1 else None))
            for db in range(DC):
                P.emit("tensor", "mm", waits=[], out=pB[:, :],
                       lhsT=wq12[:, slot, 1, db, :], rhs=v_bf[:, db, :],
                       start=(db == 0), stop=(db == DC - 1),
                       inc=(("t_f1", 1) if db == DC - 1 else None))
        P.emit("sync", "wait_only", waits=[("t_f1", P.v("t_f1")),
                                           ("s_xdump", P.v("s_xdump"))])
    elif upto == "f1l":
        weave([(phase0_gen(), 3), (mixer_gen(0), 1)])
        for hc in range(HB):
            slot = hc % 2
            wl = [("s_qw", QW_TOTAL)]
            if P.v("t_f1") >= 4:
                wl.append(("t_f1", P.v("t_f1") - 2))
            P.emit("sync", "dma", waits=wl, out=wq12[:, slot, :, :, :],
                   in_=q12s_f[hc], inc=("s_q12", 16))
        P.emit("sync", "wait_only", waits=[("s_q12", P.v("s_q12")),
                                           ("s_xdump", P.v("s_xdump"))])
    elif upto == "f1":
        weave([(phase0_gen(), 3), (mixer_gen(0), 1)])
        fg = ffn_gen(0)
        for _ in range(HB):
            next(fg)
        P.emit("sync", "wait_only", waits=[("g_gate", P.v("g_gate")),
                                           ("s_xdump", P.v("s_xdump"))])
    if upto != "full":
        return _emit_all(nc, P)
    weave([(phase0_gen(), 3), (mixer_gen(0), 1)])
    for c in range(1, nchunks):
        fg = ffn_gen(c - 1)
        for _ in range(4):
            next(fg)
        weave([(mixer_gen(c), 1), (fg, 2)])
    weave([(ffn_gen(nchunks - 1), 1)])
    P.emit("sync", "wait_only", waits=[("s_out", P.v("s_out"))])
    return _emit_all(nc, P)


def _emit_all(nc, P):
    sem_names = sorted({nm for (_, _, ws, inc, _) in P.ops
                        for nm in ([inc[0]] if inc else []) + [s for s, _ in ws]})
    sems = {}
    from contextlib import ExitStack
    with ExitStack() as ctx:
        for nm in sem_names:
            sems[nm] = ctx.enter_context(nc.semaphore(nm))

        def emit_for(eng_obj, eng_name):
            waited = {}
            for (eng, kind, ws, inc, kw) in P.ops:
                if eng != eng_name:
                    continue
                for (s, v) in ws:
                    if waited.get(s, -1) >= v:
                        continue
                    eng_obj.wait_ge(sems[s], v)
                    waited[s] = v
                if kind == "dma":
                    ins = eng_obj.dma_start(out=kw["out"], in_=kw["in_"])
                elif kind == "mm":
                    ins = eng_obj.matmul(kw["out"], kw["lhsT"], kw["rhs"],
                                         start=kw["start"], stop=kw["stop"])
                elif kind == "act":
                    a = {}
                    for f in ("bias", "scale", "accum_out"):
                        if f in kw:
                            a[f] = kw[f]
                    ins = eng_obj.activation(out=kw["out"], in_=kw["in_"],
                                             func=kw["func"], **a)
                elif kind == "sign":
                    ins = eng_obj.sign(out=kw["out"], in_=kw["in_"],
                                       bias=kw["bias"])
                elif kind == "tt":
                    ins = eng_obj.tensor_tensor(out=kw["out"], in0=kw["in0"],
                                                in1=kw["in1"], op=kw["op"])
                elif kind == "ts":
                    ins = eng_obj.tensor_single_scalar(out=kw["out"],
                                                       in_=kw["in_"],
                                                       scalar=kw["scalar"],
                                                       op=kw["op"])
                elif kind == "stt":
                    ins = eng_obj.scalar_tensor_tensor(out=kw["out"],
                                                       in0=kw["in0"],
                                                       scalar=kw["scalar"],
                                                       in1=kw["in1"],
                                                       op0=kw["op0"],
                                                       op1=kw["op1"])
                elif kind == "scan":
                    ins = eng_obj.tensor_tensor_scan(out=kw["out"],
                                                     data0=kw["d0"],
                                                     data1=kw["d1"],
                                                     initial=kw["init"],
                                                     op0=kw["op0"],
                                                     op1=kw["op1"])
                elif kind == "copy":
                    ins = eng_obj.tensor_copy(out=kw["out"], in_=kw["in_"])
                elif kind == "memset":
                    ins = eng_obj.memset(kw["ap"], kw["val"])
                elif kind == "red":
                    ins = eng_obj.reduce_sum(out=kw["out"], in_=kw["in_"],
                                             axis=AX.X)
                elif kind == "recip":
                    ins = eng_obj.reciprocal(out=kw["out"], in_=kw["in_"])
                elif kind == "nop":
                    ins = eng_obj.nop()
                elif kind == "wait_only":
                    ins = None
                else:
                    raise ValueError(kind)
                if inc is not None and ins is not None:
                    ins.then_inc(sems[inc[0]], inc[1])

        with nc.Block() as blk:
            @blk.sync
            def _(e):
                emit_for(e, "sync")

            @blk.tensor
            def _(e):
                emit_for(e, "tensor")

            @blk.scalar
            def _(e):
                emit_for(e, "scalar")

            @blk.vector
            def _(e):
                emit_for(e, "vector")

            @blk.gpsimd
            def _(e):
                emit_for(e, "gpsimd")

    return nc


def host_layout(inputs):
    x = np.ascontiguousarray(np.asarray(inputs["L0_attn_x"], np.float32))
    W_xproj = np.asarray(inputs["L0_attn_W_xproj"], np.float32)
    W_dt = np.asarray(inputs["L0_attn_W_dt"], np.float32)
    b_dt = np.asarray(inputs["L0_attn_b_dt"], np.float32)
    A_log = np.asarray(inputs["L0_attn_A_log"], np.float32)
    D_param = np.asarray(inputs["L0_attn_D_param"], np.float32)
    w1 = np.asarray(inputs["L0_ffn_w1"], np.float32)
    w2 = np.asarray(inputs["L0_ffn_w2"], np.float32)
    w3 = np.asarray(inputs["L0_ffn_w3"], np.float32)

    shared = {
        "wxT": np.ascontiguousarray(W_xproj.T),
        "wdtT": np.ascontiguousarray(W_dt.T),
        "alogT": np.ascontiguousarray(
            A_log.reshape(DC, 128, N).transpose(1, 0, 2).reshape(128, DC * N)),
        "bdtT": np.ascontiguousarray(b_dt.reshape(DC, 128).T),
        "dparT": np.ascontiguousarray(D_param.reshape(DC, 128).T),
        "w1b": np.ascontiguousarray(w1.T),
        "w2b": np.ascontiguousarray(w2.T),
        "w3b": np.ascontiguousarray(w3.T),
    }
    per_core = []
    for b in range(B):
        m = dict(shared)
        m["xT"] = np.ascontiguousarray(x[b].T)
        per_core.append(m)
    return per_core


def kernel(**inputs):
    in_maps = host_layout(inputs)
    nc = build_nc()
    res = run_bass_kernel_spmd(nc, in_maps, core_ids=list(range(B)))
    out = np.empty((B, L, D), np.float32)
    for b in range(B):
        out[b] = res.results[b]["outT"].T
    return out
